# version 03 (final)
# baseline (speedup 1.0000x reference)
import os
import numpy as np
import ml_dtypes
from contextlib import ExitStack

_LVL = int(os.environ.get("KTRUNC", "99"))
_L2 = int(os.environ.get("K2II", "99"))
_DBG = int(os.environ.get("KDBG", "0"))

import concourse.bass as bass
import concourse.mybir as mybir
from concourse.bass_utils import run_bass_kernel_spmd

F32 = mybir.dt.float32
BF16 = mybir.dt.bfloat16
AF = mybir.ActivationFunctionType
ALU = mybir.AluOpType
AX = mybir.AxisListType

NCORE = 8
BC = 2
T = 2048
H = 2048
C = 256
NCH = T // C
KT = 16
NFT = 8
EPS = 1e-5
NTOT = 16 * (T + 1) * H

R_MU, R_RST, R_NM, R_P, R_RW, R_EP, R_R2, R_NM2 = range(8)
NROWS = 8
DAT = slice(1, C + 1)


def build_nc():
    nc = bass.Bass("TRN2", target_bir_lowering=False)

    xT = nc.dram_tensor("xT", [BC, H, T], F32, kind="ExternalInput").ap()
    wzT = nc.dram_tensor("wzT", [H, H], F32, kind="ExternalInput").ap()
    whT = nc.dram_tensor("whT", [H, H], F32, kind="ExternalInput").ap()
    wupT = nc.dram_tensor("wupT", [H, H], F32, kind="ExternalInput").ap()
    wdnT = nc.dram_tensor("wdnT", [H, H], F32, kind="ExternalInput").ap()
    wex = nc.dram_tensor("wex", [H, H], F32, kind="ExternalInput").ap()
    wgT = nc.dram_tensor("wgT", [H, 8], F32, kind="ExternalInput").ap()
    bzv = nc.dram_tensor("bzv", [128, KT], F32, kind="ExternalInput").ap()
    bhv = nc.dram_tensor("bhv", [128, KT], F32, kind="ExternalInput").ap()
    lnwv = nc.dram_tensor("lnwv", [128, KT], F32, kind="ExternalInput").ap()
    lnbv = nc.dram_tensor("lnbv", [128, KT], F32, kind="ExternalInput").ap()
    h0v = nc.dram_tensor("h0v", [128, KT], F32, kind="ExternalInput").ap()
    triC = nc.dram_tensor("triC", [128, 128], F32, kind="ExternalInput").ap()
    outT = nc.dram_tensor("outT", [BC, H, T], F32, kind="ExternalOutput").ap()
    if _DBG:
        dbg_rows = nc.dram_tensor("dbg_rows", [NROWS, BC, T], F32, kind="ExternalOutput").ap()
        dbg_sc = nc.dram_tensor("dbg_sc", [1, 16], F32, kind="ExternalOutput").ap()
        dbg_lh = nc.dram_tensor("dbg_lh", [2, NFT, 128, C], F32, kind="ExternalOutput").ap()
        dbg_lh2 = nc.dram_tensor("dbg_lh2", [2, NFT, 128, C], F32, kind="ExternalOutput").ap()
        dbg_col = nc.dram_tensor("dbg_col", [128, 64], F32, kind="ExternalOutput").ap()

    d_xn = nc.dram_tensor("d_xn", [BC, NCH, KT, 128, C], BF16).ap()
    d_up = nc.dram_tensor("d_up", [BC, NCH, KT, 128, C], BF16).ap()
    d_lh = nc.dram_tensor("d_lh", [BC, 2, NCH, NFT, 128, C], F32).ap()
    d_rows = nc.dram_tensor("d_rows", [NROWS, BC, T], F32).ap()
    cc_in = nc.dram_tensor("cc_in", [1, 2], F32)
    cc_out = nc.dram_tensor("cc_out", [1, 2], F32)

    ctx = ExitStack()
    sbt = lambda name, shape, dt: ctx.enter_context(nc.sbuf_tensor(name, shape, dt))

    WBUF = sbt("WBUF", [128, KT, 2048], BF16)
    XBUF = sbt("XBUF", [128, KT, C], F32)
    XNB = sbt("XNB", [128, KT, C], BF16)
    UPB = sbt("UPB", [128, KT, C], BF16)
    LCB = sbt("LCB", [128, NFT, C], BF16)
    BS = [sbt(f"B{i}", [128, NFT, C + 1], F32) for i in range(11)]
    RS = sbt("RS", [1, 2048], F32)
    WST = sbt("WST", [128, 512], F32)
    WG = sbt("WG", [128, KT, 8], BF16)
    TRI = sbt("TRI", [128, 128], BF16)
    ONESB = sbt("ONESB", [128, 128], BF16)
    ONESCB = sbt("ONESCB", [128, 1], BF16)
    ONESRF = sbt("ONESRF", [1, 128], F32)
    ONESCF = sbt("ONESCF", [128, 1], F32)
    BZ = sbt("BZ", [128, KT], F32)
    BH = sbt("BH", [128, KT], F32)
    LNW = sbt("LNW", [128, KT], F32)
    LNB = sbt("LNB", [128, KT], F32)
    H0 = sbt("H0", [128, KT], F32)
    GL0 = sbt("GL0", [128, KT], F32)
    TT1 = sbt("TT1", [128, KT], F32)
    TT2 = sbt("TT2", [128, KT], F32)
    ACC1 = sbt("ACC1", [128, 32], F32)
    ACC2 = sbt("ACC2", [128, 32], F32)
    SCAR = sbt("SCAR", [128, NFT], F32)
    M1B = sbt("M1B", [128, KT], F32)
    M2B = sbt("M2B", [128, KT], F32)
    G8A = sbt("G8A", [128, 8], F32)
    G8B = sbt("G8B", [128, 8], F32)
    SCAL = sbt("SCAL", [128, 4], F32)
    CCS = sbt("CCS", [1, 8], F32)
    GSC = sbt("GSC", [1, 8], F32)

    p = [ctx.enter_context(nc.psum_tensor(f"p{i}", [128, 512], F32)) for i in range(8)]

    sems = {e: ctx.enter_context(nc.semaphore(f"sem{e}")) for e in "DGTVS"}
    ccsem = ctx.enter_context(nc.semaphore("ccsem"))
    selfsems = {e: ctx.enter_context(nc.semaphore(f"self{e}")) for e in "VS"}

    _DP = {
        "memset", "memzero", "tensor_copy", "tensor_tensor", "tensor_add",
        "tensor_sub", "tensor_mul", "tensor_max", "tensor_scalar",
        "tensor_single_scalar", "tensor_scalar_add", "tensor_scalar_sub",
        "tensor_scalar_mul", "tensor_scalar_max", "tensor_scalar_min",
        "tensor_relu", "tensor_reduce", "reduce_max", "reduce_sum",
        "scalar_tensor_tensor", "tensor_tensor_scan", "reciprocal", "select",
        "transpose", "bn_stats", "bn_aggr", "copy_predicated", "stream_shuffle",
        "match_replace", "activation", "sign", "square", "sqrt", "mul", "copy",
        "add",
    }

    class _SelfSync:
        def __init__(self, e, sem):
            self._e = e
            self._sem = sem
            self._n = 0

        def __getattr__(self, name):
            f = getattr(self._e, name)
            if name not in _DP:
                return f

            def wrap(*a, **k):
                ins = f(*a, **k)
                if self._n > 0:
                    ins._wait_ge(self._sem, self._n)
                ins.then_inc(self._sem, 1)
                self._n += 1
                return ins

            return wrap

    def program(owner, eng):
        if owner in "VS":
            eng = _SelfSync(eng, selfsems[owner])
        led = {e: 0 for e in "DGTVS"}

        def fin(e, ins, n=1):
            led[e] += n
            if owner == e and ins is not None:
                if e in "VS":
                    eng.wait_ge(selfsems[e], eng._n)
                    eng.sem_inc(sems[e], n)
                else:
                    ins.then_inc(sems[e], n)

        def wait(dep):
            if led[dep] > 0:
                eng.wait_ge(sems[dep], led[dep])

        def on(e):
            return owner == e

        def dma(q, out, in_):
            ins = eng.dma_start(out=out, in_=in_) if on(q) else None
            fin(q, ins, 16)

        def mm_evac(n_groups, group_mms, evac_engine, evac):
            evac_led = []
            for g in range(n_groups):
                if g >= 2 and on("T"):
                    eng.wait_ge(sems[evac_engine], evac_led[g - 2])
                ins = group_mms(g) if on("T") else None
                fin("T", ins)
                tval = led["T"]
                if on(evac_engine):
                    eng.wait_ge(sems["T"], tval)
                    ins2 = evac(g)
                else:
                    ins2 = None
                fin(evac_engine, ins2)
                evac_led.append(led[evac_engine])

        def s_seq(fns):
            last = None
            for ix, f in enumerate(fns):
                if on("S"):
                    if ix > 0:
                        eng.wait_ge(sems["S"], led["S"])
                    last = f()
                else:
                    last = None
                fin("S", last)
            return last

        for buf, src in ((BZ, bzv), (BH, bhv), (LNW, lnwv), (LNB, lnbv), (H0, h0v)):
            dma("D", buf[:], src)
        dma("D", WST[:, 0:128], triC)
        if on("V"):
            wait("D")
            eng.memset(ONESB[:], 1.0)
            eng.memset(ONESCB[:], 1.0)
            eng.memset(ONESRF[:], 1.0)
            eng.memset(ONESCF[:], 1.0)
            eng.memset(ACC1[:], 0.0)
            eng.memset(ACC2[:], 0.0)
            eng.memset(GSC[:], 0.0)
            i = eng.tensor_copy(TRI[:], WST[:, 0:128])
        else:
            i = None
        fin("V", i)
        if on("S"):
            wait("V")
        s_seq([
            lambda: eng.activation(out=TT1[:], in_=H0[:], func=AF.Abs),
            lambda: eng.activation(out=TT2[:], in_=TT1[:], func=AF.Exp, scale=-1.0),
            lambda: eng.activation(out=TT1[:], in_=TT2[:], func=AF.Ln, bias=1.0),
        ])
        if on("V"):
            wait("S")
            i = eng.scalar_tensor_tensor(TT2[:], H0[:], 0.0, TT1[:], ALU.min, ALU.subtract)
        else:
            i = None
        fin("V", i)
        if on("S"):
            wait("V")
            i = eng.activation(out=TT1[:], in_=TT2[:], func=AF.Exp)
        else:
            i = None
        fin("S", i)
        if on("V"):
            wait("S")
            eng.scalar_tensor_tensor(TT2[:], H0[:], 0.5, TT1[:], ALU.add, ALU.subtract)
            i = eng.scalar_tensor_tensor(TT2[:], TT2[:], 0.0, TT1[:], ALU.max, ALU.add)
        else:
            i = None
        fin("V", i)
        if on("S"):
            wait("V")
            i = eng.activation(out=GL0[:], in_=TT2[:], func=AF.Ln)
        else:
            i = None
        fin("S", i)

        halves = [] if _LVL < 2 else ([0] if _LVL < 4 else [0, 1])
        for half in halves:
            hsl = slice(half * 1024, half * 1024 + 1024)
            for wi, wsrc in enumerate((wzT, whT)):
                for kt in range(KT):
                    for c2 in range(2):
                        if on("D"):
                            wait("V")
                            wait("T")
                        dma("D", WST[:], wsrc[kt * 128:(kt + 1) * 128,
                                              half * 1024 + c2 * 512:half * 1024 + c2 * 512 + 512])
                        if on("V"):
                            wait("D")
                            i = eng.tensor_copy(
                                WBUF[:, kt, wi * 1024 + c2 * 512:wi * 1024 + c2 * 512 + 512],
                                WST[:])
                        else:
                            i = None
                        fin("V", i)
            for b in ([0] if _LVL < 4 else range(BC)):
                if on("V"):
                    wait("S")
                    eng.memset(BS[5][:, :, 0:1], 0.0)
                    eng.memset(SCAR[:], 1.0)
                    i = eng.tensor_copy(BS[8][:, :, 0:1], GL0[:, half * 8:half * 8 + 8])
                else:
                    i = None
                fin("V", i)
                for ch in ([0] if _LVL < 4 else range(NCH)):
                    csl = slice(ch * C, ch * C + C)
                    cidx = (half * BC + b) * NCH + ch
                    if _LVL < 3:
                        continue
                    if on("D"):
                        wait("V")
                        wait("S")
                        wait("T")
                        wait("G")
                    if half == 0:
                        dma("D", XBUF[:], xT[b, :, csl].rearrange("(kt q) c -> q kt c", q=128))
                    else:
                        dma("D", XNB[:], d_xn[b, ch].rearrange("kt q c -> q kt c"))
                        dma("D", RS[0:1, 0:C], d_rows[R_P, b:b + 1, csl])
                    if half == 0:
                        if on("V"):
                            wait("D")
                            wait("G")
                            i = eng.tensor_copy(XNB[:], XBUF[:])
                        else:
                            i = None
                        fin("V", i)
                        if on("S"):
                            wait("D")
                            i = eng.activation(out=UPB[:], in_=XBUF[:], func=AF.Square)
                        else:
                            i = None
                        fin("S", i)
                        if on("T"):
                            wait("V")
                            wait("S")
                            for kt in range(KT):
                                eng.matmul(p[6][0:1, 0:C], ONESCB[:], XNB[:, kt, :],
                                           start=(kt == 0), stop=(kt == KT - 1))
                            for kt in range(KT):
                                i = eng.matmul(p[6][0:1, C:2 * C], ONESCB[:], UPB[:, kt, :],
                                               start=(kt == 0), stop=(kt == KT - 1))
                        else:
                            i = None
                        fin("T", i)
                        if on("V"):
                            wait("T")
                            eng.tensor_scalar(RS[0:1, 0:C], p[6][0:1, 0:C], 1.0 / H, None, ALU.mult)
                            eng.tensor_scalar(RS[0:1, C:2 * C], p[6][0:1, C:2 * C], 1.0 / H, None, ALU.mult)
                            eng.tensor_mul(RS[0:1, 2 * C:3 * C], RS[0:1, 0:C], RS[0:1, 0:C])
                            eng.tensor_sub(RS[0:1, C:2 * C], RS[0:1, C:2 * C], RS[0:1, 2 * C:3 * C])
                            i = eng.tensor_scalar(RS[0:1, C:2 * C], RS[0:1, C:2 * C], EPS, None, ALU.add)
                        else:
                            i = None
                        fin("V", i)
                        if on("S"):
                            wait("V")
                        s_seq([
                            lambda: eng.activation(out=RS[0:1, 2 * C:3 * C], in_=RS[0:1, C:2 * C],
                                                   func=AF.Ln),
                            lambda: eng.activation(out=RS[0:1, C:2 * C], in_=RS[0:1, 2 * C:3 * C],
                                                   func=AF.Exp, scale=-0.5),
                        ])
                        if on("V"):
                            wait("S")
                            i = eng.scalar_tensor_tensor(RS[0:1, 2 * C:3 * C], RS[0:1, 0:C], -1.0,
                                                         RS[0:1, C:2 * C], ALU.mult, ALU.mult)
                        else:
                            i = None
                        fin("V", i)
                        for ridx, roff in ((R_MU, 0), (R_RST, C), (R_NM, 2 * C)):
                            if on("G"):
                                wait("V")
                            dma("G", d_rows[ridx, b:b + 1, csl], RS[0:1, roff:roff + C])
                        if on("T"):
                            wait("V")
                            eng.matmul(p[7][:, 0:C], ONESRF[:], RS[0:1, C:2 * C],
                                       start=True, stop=True)
                            i = eng.matmul(p[7][:, C:2 * C], ONESRF[:], RS[0:1, 2 * C:3 * C],
                                           start=True, stop=True)
                        else:
                            i = None
                        fin("T", i)
                        if on("V"):
                            wait("T")
                            for kt in range(KT):
                                eng.tensor_tensor(BS[1 + kt // 8][:, kt % 8, DAT],
                                                  XBUF[:, kt, :], p[7][:, 0:C], ALU.mult)
                            for kt in range(KT):
                                i = eng.tensor_tensor(XNB[:, kt, :],
                                                      BS[1 + kt // 8][:, kt % 8, DAT],
                                                      p[7][:, C:2 * C], ALU.add)
                        else:
                            i = None
                        fin("V", i)
                        if on("G"):
                            wait("V")
                        dma("G", d_xn[b, ch].rearrange("kt q c -> q kt c"), XNB[:])

                    K_, U_, TA, TB = BS[3], BS[4], BS[1], BS[2]
                    AL, DD, VV, ML, SL, LH = BS[5], BS[6], BS[7], BS[8], BS[9], BS[10]

                    def kgroup(g):
                        for kt in range(KT):
                            i = eng.matmul(p[g % 2][:, 0:C],
                                           WBUF[:, kt, g * 128:(g + 1) * 128],
                                           XNB[:, kt, :],
                                           start=(kt == 0), stop=(kt == KT - 1))
                        return i

                    def kevac(g):
                        return eng.activation(out=K_[:, g, DAT], in_=p[g % 2][:, 0:C],
                                              func=AF.Identity,
                                              bias=BZ[:, half * 8 + g:half * 8 + g + 1])

                    def ugroup(g):
                        for kt in range(KT):
                            i = eng.matmul(p[2 + g % 2][:, 0:C],
                                           WBUF[:, kt, 1024 + g * 128:1024 + (g + 1) * 128],
                                           XNB[:, kt, :],
                                           start=(kt == 0), stop=(kt == KT - 1))
                        return i

                    def uevac(g):
                        return eng.activation(out=U_[:, g, DAT], in_=p[2 + g % 2][:, 0:C],
                                              func=AF.Identity,
                                              bias=BH[:, half * 8 + g:half * 8 + g + 1])

                    if on("T"):
                        wait("V")
                        wait("D")
                        wait("S")
                    mm_evac(NFT, kgroup, "S", kevac)
                    mm_evac(NFT, ugroup, "S", uevac)
                    if on("S"):
                        eng.activation(out=TA[:, :, DAT], in_=K_[:, :, DAT], func=AF.Abs)
                        eng.activation(out=TB[:, :, DAT], in_=TA[:, :, DAT], func=AF.Exp, scale=-1.0)
                        i = eng.activation(out=TA[:, :, DAT], in_=TB[:, :, DAT], func=AF.Ln, bias=1.0)
                    else:
                        i = None
                    fin("S", i)
                    if on("V"):
                        wait("S")
                        eng.scalar_tensor_tensor(LCB[:], K_[:, :, DAT], 0.0, TA[:, :, DAT],
                                                 ALU.max, ALU.add)
                        i = eng.tensor_sub(K_[:, :, DAT], K_[:, :, DAT], LCB[:])
                    else:
                        i = None
                    fin("V", i)
                    if on("S"):
                        wait("V")
                        eng.activation(out=TA[:, :, DAT], in_=U_[:, :, DAT], func=AF.Abs)
                        eng.activation(out=TB[:, :, DAT], in_=TA[:, :, DAT], func=AF.Exp, scale=-1.0)
                        i = eng.activation(out=TA[:, :, DAT], in_=TB[:, :, DAT], func=AF.Ln, bias=1.0)
                    else:
                        i = None
                    fin("S", i)
                    if on("V"):
                        wait("S")
                        i = eng.scalar_tensor_tensor(TA[:, :, DAT], U_[:, :, DAT], 0.0,
                                                     TA[:, :, DAT], ALU.min, ALU.subtract)
                    else:
                        i = None
                    fin("V", i)
                    if on("S"):
                        wait("V")
                        i = eng.activation(out=TB[:, :, DAT], in_=TA[:, :, DAT], func=AF.Exp)
                    else:
                        i = None
                    fin("S", i)
                    if on("V"):
                        wait("S")
                        eng.scalar_tensor_tensor(U_[:, :, DAT], U_[:, :, DAT], 0.5,
                                                 TB[:, :, DAT], ALU.add, ALU.subtract)
                        i = eng.scalar_tensor_tensor(U_[:, :, DAT], U_[:, :, DAT], 0.0,
                                                     TB[:, :, DAT], ALU.max, ALU.add)
                    else:
                        i = None
                    fin("V", i)
                    if on("S"):
                        wait("V")
                        i = eng.activation(out=TB[:, :, DAT], in_=U_[:, :, DAT], func=AF.Ln)
                    else:
                        i = None
                    fin("S", i)
                    if on("V"):
                        wait("S")
                        i = eng.tensor_add(K_[:, :, DAT], K_[:, :, DAT], TB[:, :, DAT])
                    else:
                        i = None
                    fin("V", i)
                    if half == 1:
                        if on("T"):
                            wait("V")
                            wait("D")
                            i = eng.matmul(p[7][:, 0:C], ONESRF[:], RS[0:1, 0:C],
                                           start=True, stop=True)
                        else:
                            i = None
                        fin("T", i)
                        if on("V"):
                            wait("T")
                            for j in range(NFT):
                                i = eng.tensor_tensor(K_[:, j, DAT], K_[:, j, DAT],
                                                      p[7][:, 0:C], ALU.add)
                        else:
                            i = None
                        fin("V", i)

                    def agroup(g):
                        for j in range(g):
                            eng.matmul(p[4 + g % 2][:, 0:C], ONESB[:], LCB[:, j, :],
                                       start=(j == 0), stop=False)
                        return eng.matmul(p[4 + g % 2][:, 0:C], TRI[:], LCB[:, g, :],
                                          start=(g == 0), stop=True)

                    def aevac(g):
                        return eng.copy(out=AL[:, g, DAT], in_=p[4 + g % 2][:, 0:C])

                    if on("T"):
                        wait("V")
                        wait("S")
                    mm_evac(NFT, agroup, "S", aevac)
                    if half == 0:
                        if on("T"):
                            wait("V")
                            for j in range(NFT):
                                i = eng.matmul(p[6][0:1, 0:C], ONESCB[:], LCB[:, j, :],
                                               start=(j == 0), stop=(j == NFT - 1))
                        else:
                            i = None
                        fin("T", i)
                        if on("V"):
                            wait("T")
                            i = eng.tensor_copy(RS[0:1, 3 * C:4 * C], p[6][0:1, 0:C])
                        else:
                            i = None
                        fin("V", i)
                        if on("G"):
                            wait("V")
                        dma("G", d_rows[R_P, b:b + 1, csl], RS[0:1, 3 * C:4 * C])
                    if on("V"):
                        wait("S")
                        i = eng.tensor_sub(DD[:, :, DAT], AL[:, :, 0:C], AL[:, :, 1:C + 1])
                    else:
                        i = None
                    fin("V", i)
                    if on("V"):
                        for j in range(NFT):
                            i = eng.tensor_tensor_scan(ML[:, j, DAT], DD[:, j, DAT],
                                                       K_[:, j, DAT], ML[:, j, 0:1],
                                                       ALU.add, ALU.max)
                    else:
                        i = None
                    fin("V", i)
                    if on("V"):
                        eng.tensor_sub(VV[:, :, DAT], ML[:, :, 0:C], ML[:, :, 1:C + 1])
                        i = eng.tensor_add(VV[:, :, DAT], VV[:, :, DAT], DD[:, :, DAT])
                    else:
                        i = None
                    fin("V", i)
                    if on("S"):
                        wait("V")
                        i = eng.activation(out=DD[:, :, DAT], in_=VV[:, :, DAT], func=AF.Exp)
                    else:
                        i = None
                    fin("S", i)
                    if on("V"):
                        i = eng.tensor_sub(K_[:, :, DAT], K_[:, :, DAT], ML[:, :, 1:C + 1])
                    else:
                        i = None
                    fin("V", i)
                    if on("S"):
                        wait("V")
                        i = eng.activation(out=VV[:, :, DAT], in_=K_[:, :, DAT], func=AF.Exp)
                    else:
                        i = None
                    fin("S", i)
                    if on("V"):
                        wait("S")
                        for j in range(NFT):
                            eng.tensor_tensor_scan(SL[:, j, DAT], DD[:, j, DAT],
                                                   VV[:, j, DAT], SCAR[:, j:j + 1],
                                                   ALU.mult, ALU.add)
                        eng.tensor_copy(SCAR[:], SL[:, :, C:C + 1])
                        eng.tensor_copy(AL[:, :, 0:1], AL[:, :, C:C + 1])
                        i = eng.tensor_copy(ML[:, :, 0:1], ML[:, :, C:C + 1])
                    else:
                        i = None
                    fin("V", i)
                    if on("S"):
                        wait("V")
                        i = eng.activation(out=DD[:, :, DAT], in_=SL[:, :, DAT], func=AF.Ln)
                    else:
                        i = None
                    fin("S", i)
                    if on("V"):
                        wait("S")
                        wait("G")
                        i = eng.scalar_tensor_tensor(LH[:, :, DAT], ML[:, :, 1:C + 1], 0.0,
                                                     DD[:, :, DAT], ALU.add, ALU.add,
                                                     accum_out=ACC1[:, cidx:cidx + 1])
                    else:
                        i = None
                    fin("V", i)
                    if on("S"):
                        wait("V")
                        i = eng.activation(out=VV[:, :, DAT], in_=LH[:, :, DAT], func=AF.Square,
                                           accum_out=ACC2[:, cidx:cidx + 1])
                    else:
                        i = None
                    fin("S", i)
                    if half == 1:
                        if on("V"):
                            i = eng.tensor_copy(LCB[:], LH[:, :, DAT])
                        else:
                            i = None
                        fin("V", i)
                        if on("T"):
                            wait("V")
                            for j in range(NFT):
                                i = eng.matmul(p[6][0:1, 0:C], ONESCB[:], LCB[:, j, :],
                                               start=(j == 0), stop=(j == NFT - 1))
                        else:
                            i = None
                        fin("T", i)
                        if on("V"):
                            wait("T")
                            eng.tensor_mul(RS[0:1, C:2 * C], RS[0:1, 0:C], p[6][0:1, 0:C])
                            eng.reduce_sum(RS[0:1, 4 * C:4 * C + 1], RS[0:1, C:2 * C], axis=AX.X)
                            i = eng.tensor_add(GSC[0:1, 2:3], GSC[0:1, 2:3],
                                               RS[0:1, 4 * C:4 * C + 1])
                        else:
                            i = None
                        fin("V", i)
                    if on("G"):
                        wait("V")
                        wait("S")
                    dma("G", d_lh[b, half, ch].rearrange("ft q c -> q ft c"), LH[:, :, DAT])

        if _LVL < 5:
            if on("G"):
                eng.wait_ge(sems["G"], led["G"])
            return led
        if on("V"):
            wait("S")
            eng.reduce_sum(TT1[:, 0:1], ACC1[:], axis=AX.X)
            eng.reduce_sum(TT1[:, 1:2], ACC2[:], axis=AX.X)
            eng.reduce_sum(TT1[:, 2:3], GL0[:], axis=AX.X)
            eng.tensor_mul(TT2[:], GL0[:], GL0[:])
            eng.reduce_sum(TT1[:, 3:4], TT2[:], axis=AX.X)
            eng.scalar_tensor_tensor(TT1[:, 0:1], TT1[:, 2:3], 2.0, TT1[:, 0:1],
                                     ALU.mult, ALU.add)
            i = eng.scalar_tensor_tensor(TT1[:, 1:2], TT1[:, 3:4], 2.0, TT1[:, 1:2],
                                         ALU.mult, ALU.add)
        else:
            i = None
        fin("V", i)
        if on("T"):
            wait("V")
            eng.matmul(p[6][0:1, 0:1], ONESCF[:], TT1[:, 0:1], start=True, stop=True)
            i = eng.matmul(p[6][0:1, 1:2], ONESCF[:], TT1[:, 1:2], start=True, stop=True)
        else:
            i = None
        fin("T", i)
        for b in range(BC):
            if on("D"):
                wait("V")
            dma("D", RS[0:1, 0:T], d_rows[R_P, b:b + 1, :])
            if on("V"):
                wait("D")
                wait("T")
                eng.reduce_sum(GSC[0:1, 5:6], RS[0:1, 0:T], axis=AX.X)
                eng.tensor_add(GSC[0:1, 3:4], GSC[0:1, 3:4], GSC[0:1, 5:6])
                eng.tensor_mul(RS[0:1, 0:T], RS[0:1, 0:T], RS[0:1, 0:T])
                eng.reduce_sum(GSC[0:1, 5:6], RS[0:1, 0:T], axis=AX.X)
                i = eng.tensor_add(GSC[0:1, 4:5], GSC[0:1, 4:5], GSC[0:1, 5:6])
            else:
                i = None
            fin("V", i)
        if on("V"):
            eng.scalar_tensor_tensor(CCS[0:1, 0:1], GSC[0:1, 3:4], -1024.0, p[6][0:1, 0:1],
                                     ALU.mult, ALU.add)
            eng.scalar_tensor_tensor(CCS[0:1, 1:2], GSC[0:1, 2:3], -2.0, p[6][0:1, 1:2],
                                     ALU.mult, ALU.add)
            i = eng.scalar_tensor_tensor(CCS[0:1, 1:2], GSC[0:1, 4:5], 1024.0, CCS[0:1, 1:2],
                                         ALU.mult, ALU.add)
        else:
            i = None
        fin("V", i)
        if on("G"):
            wait("V")
        dma("G", cc_in.ap(), CCS[0:1, 0:2])
        if on("G"):
            wait("G")
            eng.collective_compute(
                "AllReduce", ALU.add, replica_groups=[list(range(NCORE))],
                ins=[cc_in.ap()], outs=[cc_out.ap()],
            ).then_inc(ccsem)

        if _LVL < 6:
            if on("G"):
                eng.wait_ge(sems["G"], led["G"])
                eng.wait_ge(ccsem, 1)
            return led
        for kt in range(KT):
            for wh in range(4):
                if on("D"):
                    wait("V")
                    wait("T")
                dma("D", WST[:], wupT[kt * 128:(kt + 1) * 128, wh * 512:(wh + 1) * 512])
                if on("V"):
                    wait("D")
                    i = eng.tensor_copy(WBUF[:, kt, wh * 512:(wh + 1) * 512], WST[:])
                else:
                    i = None
                fin("V", i)
        if on("D"):
            wait("V")
        dma("D", WST[:, 0:128].rearrange("q (kt e) -> q kt e", kt=16),
            wgT.rearrange("(kt q) e -> q kt e", q=128))
        if on("V"):
            wait("D")
            i = eng.tensor_copy(WG[:], WST[:, 0:128].rearrange("q (kt e) -> q kt e", kt=16))
        else:
            i = None
        fin("V", i)
        for b in range(BC):
            for ch in range(NCH):
                if on("D"):
                    wait("T")
                    wait("V")
                    wait("G")
                dma("D", XNB[:], d_xn[b, ch].rearrange("kt q c -> q kt c"))

                def upgroup(g):
                    for kt in range(KT):
                        i = eng.matmul(p[g % 2][:, 0:C],
                                       WBUF[:, kt, g * 128:(g + 1) * 128],
                                       XNB[:, kt, :],
                                       start=(kt == 0), stop=(kt == KT - 1))
                    return i

                def upevac(g):
                    return eng.activation(out=UPB[:, g, :], in_=p[g % 2][:, 0:C],
                                          func=AF.Relu)

                if on("T"):
                    wait("D")
                    wait("G")
                    wait("V")
                    wait("S")
                mm_evac(KT, upgroup, "S", upevac)
                if on("G"):
                    wait("S")
                dma("G", d_up[b, ch].rearrange("kt q c -> q kt c"), UPB[:])
                for tt in range(2):
                    if on("T"):
                        wait("V")
                        for kt in range(KT):
                            i = eng.matmul(p[4][0:128, tt * 8:tt * 8 + 8],
                                           XNB[:, kt, tt * 128:(tt + 1) * 128],
                                           WG[:, kt, :],
                                           start=(kt == 0), stop=(kt == KT - 1))
                    else:
                        i = None
                    fin("T", i)
                    if on("V"):
                        wait("T")
                        tcol = ch * 2 + tt
                        eng.tensor_copy(G8A[:], p[4][0:128, tt * 8:tt * 8 + 8])
                        eng.reduce_max(M1B[:, tcol:tcol + 1], G8A[:], axis=AX.X)
                        eng.tensor_scalar(G8B[:], G8A[:], M1B[:, tcol:tcol + 1], -1.0,
                                          ALU.subtract, ALU.mult)
                        eng.tensor_scalar_min(G8B[:], G8B[:], 1e-15)
                        eng.tensor_scalar(G8B[:], G8B[:], 1e30, -1e15, ALU.mult, ALU.add)
                        eng.tensor_add(G8B[:], G8B[:], G8A[:])
                        i = eng.reduce_max(M2B[:, tcol:tcol + 1], G8B[:], axis=AX.X)
                    else:
                        i = None
                    fin("V", i)
            if on("V"):
                i = eng.tensor_sub(M1B[:], M1B[:], M2B[:])
            else:
                i = None
            fin("V", i)
            if on("S"):
                wait("V")
                i = eng.activation(out=M2B[:], in_=M1B[:], func=AF.Exp, scale=-1.0)
            else:
                i = None
            fin("S", i)
            if on("V"):
                wait("S")
                eng.tensor_scalar(M1B[:], M2B[:], 1.0, None, ALU.add)
                eng.reciprocal(TT1[:], M1B[:])
                i = eng.tensor_mul(M1B[:], M1B[:], TT1[:])
            else:
                i = None
            fin("V", i)
            for tcol in range(KT):
                if on("G"):
                    wait("V")
                dma("G", d_rows[R_RW, b, tcol * 128:(tcol + 1) * 128]
                    .rearrange("(q one) -> q one", one=1),
                    M1B[:, tcol:tcol + 1])

        if _LVL < 7:
            if on("G"):
                eng.wait_ge(sems["G"], led["G"])
                eng.wait_ge(ccsem, 1)
            return led
        if on("G"):
            eng.wait_ge(ccsem, 1)
        dma("G", GSC[0:1, 0:2], cc_out.ap())
        if on("V"):
            wait("G")
            eng.tensor_scalar(GSC[0:1, 5:6], GSC[0:1, 0:1], 1.0 / NTOT, None, ALU.mult)
            eng.tensor_scalar(GSC[0:1, 6:7], GSC[0:1, 1:2], 1.0 / (NTOT - 1), None, ALU.mult)
            eng.tensor_mul(GSC[0:1, 7:8], GSC[0:1, 5:6], GSC[0:1, 5:6])
            eng.tensor_scalar(GSC[0:1, 7:8], GSC[0:1, 7:8], NTOT / (NTOT - 1.0), None, ALU.mult)
            i = eng.tensor_sub(GSC[0:1, 6:7], GSC[0:1, 6:7], GSC[0:1, 7:8])
        else:
            i = None
        fin("V", i)
        if on("S"):
            wait("V")
        s_seq([
            lambda: eng.activation(out=GSC[0:1, 7:8], in_=GSC[0:1, 6:7], func=AF.Ln),
            lambda: eng.activation(out=GSC[0:1, 6:7], in_=GSC[0:1, 7:8], func=AF.Exp, scale=-0.5),
        ])
        if on("V"):
            wait("S")
            eng.scalar_tensor_tensor(GSC[0:1, 7:8], GSC[0:1, 5:6], -1.0, GSC[0:1, 6:7],
                                     ALU.mult, ALU.mult)
            i = eng.tensor_scalar(GSC[0:1, 4:5], GSC[0:1, 6:7], -1.0, None, ALU.mult)
        else:
            i = None
        fin("V", i)
        if on("T"):
            wait("V")
            eng.matmul(p[7][:, 0:1], ONESRF[:], GSC[0:1, 6:7], start=True, stop=True)
            i = eng.matmul(p[7][:, 1:2], ONESRF[:], GSC[0:1, 7:8], start=True, stop=True)
        else:
            i = None
        fin("T", i)
        if on("S"):
            wait("T")
            eng.copy(out=SCAL[:, 0:1], in_=p[7][:, 0:1])
            i = eng.copy(out=SCAL[:, 1:2], in_=p[7][:, 1:2])
        else:
            i = None
        fin("S", i)

        if _LVL < 8:
            if on("G"):
                eng.wait_ge(sems["G"], led["G"])
            return led
        for b in range(BC):
            for ch in range(NCH):
                csl = slice(ch * C, ch * C + C)
                if on("D"):
                    wait("V")
                    wait("G")
                dma("D", RS[0:1, 0:C], d_rows[R_P, b:b + 1, csl])
                dma("D", RS[0:1, 6 * C:7 * C], d_rows[R_MU, b:b + 1, csl])
                dma("D", RS[0:1, 7 * C:8 * C], d_rows[R_RST, b:b + 1, csl])
                if on("T"):
                    wait("D")
                    wait("S")
                    wait("V")
                    i = eng.matmul(p[7][:, 0:C], ONESRF[:], RS[0:1, 0:C],
                                   start=True, stop=True)
                else:
                    i = None
                fin("T", i)
                for hf in range(2):
                    if on("D"):
                        wait("T")
                        wait("V")
                        wait("S")
                    dma("D", BS[10][:, :, DAT], d_lh[b, hf, ch].rearrange("ft q c -> q ft c"))
                    dma("D", XBUF[:, 0:8, :],
                        xT[b, hf * 1024:(hf + 1) * 1024, csl].rearrange("(ft q) c -> q ft c", q=128))
                    if hf == 1:
                        if on("V"):
                            wait("D")
                            wait("T")
                            for j in range(NFT):
                                i = eng.tensor_tensor(BS[10][:, j, DAT], BS[10][:, j, DAT],
                                                      p[7][:, 0:C], ALU.subtract)
                        else:
                            i = None
                        fin("V", i)
                    if on("S"):
                        wait("D")
                        wait("V")
                        i = eng.activation(out=XNB[:, 0:8, :], in_=BS[10][:, :, DAT],
                                           func=AF.Exp, scale=SCAL[:, 0:1], bias=SCAL[:, 1:2])
                    else:
                        i = None
                    fin("S", i)
                    if on("V"):
                        wait("S")
                        eng.tensor_mul(UPB[:, 0:8, :], XNB[:, 0:8, :], XBUF[:, 0:8, :])
                        i = eng.tensor_mul(XNB[:, 8:16, :], XNB[:, 0:8, :], XNB[:, 0:8, :])
                    else:
                        i = None
                    fin("V", i)
                    if on("T"):
                        wait("V")
                        wait("S")
                        for j in range(NFT):
                            eng.matmul(p[4][0:1, hf * C:hf * C + C], ONESCB[:], XNB[:, j, :],
                                       start=(j == 0), stop=(j == NFT - 1))
                        for j in range(NFT):
                            eng.matmul(p[5][0:1, hf * C:hf * C + C], ONESCB[:],
                                       XNB[:, 8 + j, :],
                                       start=(j == 0), stop=(j == NFT - 1))
                        for j in range(NFT):
                            i = eng.matmul(p[6][0:1, hf * C:hf * C + C], ONESCB[:], UPB[:, j, :],
                                           start=(j == 0), stop=(j == NFT - 1))
                    else:
                        i = None
                    fin("T", i)
                if on("V"):
                    wait("T")
                    eng.tensor_copy(RS[0:1, 2 * C:3 * C], p[4][0:1, 0:C])
                    eng.tensor_tensor(RS[0:1, 3 * C:4 * C], RS[0:1, 2 * C:3 * C],
                                      p[4][0:1, C:2 * C], ALU.add)
                    eng.tensor_copy(RS[0:1, 2 * C:3 * C], p[5][0:1, 0:C])
                    eng.tensor_tensor(RS[0:1, 4 * C:5 * C], RS[0:1, 2 * C:3 * C],
                                      p[5][0:1, C:2 * C], ALU.add)
                    eng.tensor_copy(RS[0:1, 2 * C:3 * C], p[6][0:1, 0:C])
                    i = eng.tensor_tensor(RS[0:1, 5 * C:6 * C], RS[0:1, 2 * C:3 * C],
                                          p[6][0:1, C:2 * C], ALU.add)
                else:
                    i = None
                fin("V", i)
                if on("V"):
                    eng.tensor_scalar(RS[0:1, 2 * C:3 * C], RS[0:1, 3 * C:4 * C], 1.0 / H,
                                      None, ALU.mult)
                    eng.tensor_mul(RS[0:1, C:2 * C], RS[0:1, 6 * C:7 * C], RS[0:1, 3 * C:4 * C])
                    eng.tensor_sub(RS[0:1, 5 * C:6 * C], RS[0:1, 5 * C:6 * C], RS[0:1, C:2 * C])
                    eng.tensor_mul(RS[0:1, 5 * C:6 * C], RS[0:1, 5 * C:6 * C],
                                   RS[0:1, 7 * C:8 * C])
                    eng.tensor_mul(RS[0:1, C:2 * C], RS[0:1, 7 * C:8 * C], RS[0:1, 7 * C:8 * C])
                    eng.tensor_scalar(RS[0:1, C:2 * C], RS[0:1, C:2 * C], -EPS * H, float(H),
                                      ALU.mult, ALU.add)
                    eng.scalar_tensor_tensor(RS[0:1, 4 * C:5 * C], RS[0:1, 5 * C:6 * C], 2.0,
                                             RS[0:1, 4 * C:5 * C], ALU.mult, ALU.add)
                    eng.tensor_add(RS[0:1, 4 * C:5 * C], RS[0:1, 4 * C:5 * C], RS[0:1, C:2 * C])
                    eng.tensor_scalar(RS[0:1, 4 * C:5 * C], RS[0:1, 4 * C:5 * C], 1.0 / H,
                                      None, ALU.mult)
                    eng.tensor_mul(RS[0:1, C:2 * C], RS[0:1, 2 * C:3 * C], RS[0:1, 2 * C:3 * C])
                    eng.tensor_sub(RS[0:1, 4 * C:5 * C], RS[0:1, 4 * C:5 * C],
                                   RS[0:1, C:2 * C])
                    i = eng.tensor_scalar(RS[0:1, 4 * C:5 * C], RS[0:1, 4 * C:5 * C], EPS,
                                          None, ALU.add)
                else:
                    i = None
                fin("V", i)
                if on("S"):
                    wait("V")
                s_seq([
                    lambda: eng.activation(out=RS[0:1, C:2 * C], in_=RS[0:1, 4 * C:5 * C],
                                           func=AF.Ln),
                    lambda: eng.activation(out=RS[0:1, 4 * C:5 * C], in_=RS[0:1, C:2 * C],
                                           func=AF.Exp, scale=-0.5),
                ])
                if on("V"):
                    wait("S")
                    i = eng.scalar_tensor_tensor(RS[0:1, 3 * C:4 * C], RS[0:1, 2 * C:3 * C],
                                                 -1.0, RS[0:1, 4 * C:5 * C],
                                                 ALU.mult, ALU.mult)
                else:
                    i = None
                fin("V", i)
                if on("G"):
                    wait("V")
                    wait("S")
                dma("G", d_rows[R_R2, b:b + 1, csl], RS[0:1, 4 * C:5 * C])
                dma("G", d_rows[R_NM2, b:b + 1, csl], RS[0:1, 3 * C:4 * C])

        if _LVL < 9:
            if on("G"):
                eng.wait_ge(sems["G"], led["G"])
            return led
        for half in range(2):
            hsl = slice(half * 1024, half * 1024 + 1024)
            for wi, wsrc in enumerate((wdnT, wex)):
                for kt in range(KT):
                    for c2 in range(2):
                        if on("D"):
                            wait("V")
                            wait("T")
                        dma("D", WST[:], wsrc[kt * 128:(kt + 1) * 128,
                                              half * 1024 + c2 * 512:half * 1024 + c2 * 512 + 512])
                        if on("V"):
                            wait("D")
                            i = eng.tensor_copy(
                                WBUF[:, kt, wi * 1024 + c2 * 512:wi * 1024 + c2 * 512 + 512],
                                WST[:])
                        else:
                            i = None
                        fin("V", i)
            for b in (range(BC) if _L2 >= 10 else [0]):
                for ch in (range(NCH) if _L2 >= 10 else [0]):
                    csl = slice(ch * C, ch * C + C)
                    if _L2 < 2:
                        continue
                    if on("D"):
                        wait("T")
                        wait("V")
                        wait("S")
                        wait("G")
                    dma("D", BS[10][:, :, DAT], d_lh[b, half, ch].rearrange("ft q c -> q ft c"))
                    dma("D", XBUF[:, 0:8, :],
                        xT[b, hsl, csl].rearrange("(ft q) c -> q ft c", q=128))
                    dma("D", XNB[:], d_xn[b, ch].rearrange("kt q c -> q kt c"))
                    dma("D", UPB[:], d_up[b, ch].rearrange("kt q c -> q kt c"))
                    dma("D", RS[0:1, 0:C], d_rows[R_RST, b:b + 1, csl])
                    dma("D", RS[0:1, C:2 * C], d_rows[R_NM, b:b + 1, csl])
                    dma("D", RS[0:1, 2 * C:3 * C], d_rows[R_R2, b:b + 1, csl])
                    dma("D", RS[0:1, 3 * C:4 * C], d_rows[R_NM2, b:b + 1, csl])
                    dma("D", RS[0:1, 4 * C:5 * C], d_rows[R_RW, b:b + 1, csl])
                    if half == 1:
                        dma("D", RS[0:1, 5 * C:6 * C], d_rows[R_P, b:b + 1, csl])
                    if _L2 < 3:
                        continue
                    nbc = 6 if half == 1 else 5
                    for bi in range(nbc):
                        if on("T"):
                            wait("D")
                            if bi >= 2:
                                wait("S")
                            i = eng.matmul(p[4 + bi % 2][:, 0:C], ONESRF[:],
                                           RS[0:1, bi * C:bi * C + C], start=True, stop=True)
                        else:
                            i = None
                        fin("T", i)
                        if on("S"):
                            wait("T")
                            i = eng.copy(out=BS[0][:, bi, 0:C],
                                         in_=p[4 + bi % 2][:, 0:C])
                        else:
                            i = None
                        fin("S", i)
                    if _L2 < 4:
                        continue
                    if half == 1:
                        if on("V"):
                            wait("D")
                            wait("S")
                            for j in range(NFT):
                                i = eng.tensor_tensor(BS[10][:, j, DAT], BS[10][:, j, DAT],
                                                      BS[0][:, 5, 0:C], ALU.subtract)
                        else:
                            i = None
                        fin("V", i)
                    if on("S"):
                        wait("D")
                        wait("V")
                        i = eng.activation(out=BS[4][:, :, DAT], in_=BS[10][:, :, DAT],
                                           func=AF.Exp, scale=SCAL[:, 0:1], bias=SCAL[:, 1:2])
                    else:
                        i = None
                    fin("S", i)
                    if on("V"):
                        wait("S")
                        for j in range(NFT):
                            eng.tensor_tensor(BS[5][:, j, DAT], XBUF[:, j, :],
                                              BS[0][:, 0, 0:C], ALU.mult)
                        for j in range(NFT):
                            eng.tensor_tensor(BS[5][:, j, DAT], BS[5][:, j, DAT],
                                              BS[0][:, 1, 0:C], ALU.add)
                        eng.tensor_add(BS[4][:, :, DAT], BS[4][:, :, DAT], BS[5][:, :, DAT])
                        for j in range(NFT):
                            eng.tensor_tensor(BS[6][:, j, DAT], BS[4][:, j, DAT],
                                              BS[0][:, 2, 0:C], ALU.mult)
                        for j in range(NFT):
                            i = eng.tensor_tensor(BS[6][:, j, DAT], BS[6][:, j, DAT],
                                                  BS[0][:, 3, 0:C], ALU.add)
                    else:
                        i = None
                    fin("V", i)
                    if on("S"):
                        wait("V")
                        for j in range(NFT):
                            col = half * 8 + j
                            i = eng.activation(out=BS[7][:, j, DAT], in_=BS[6][:, j, DAT],
                                               func=AF.Identity, scale=LNW[:, col:col + 1],
                                               bias=LNB[:, col:col + 1])
                    else:
                        i = None
                    fin("S", i)

                    def egroup(g):
                        for kt in range(KT):
                            i = eng.matmul(p[g % 2][:, 0:C],
                                           WBUF[:, kt, 1024 + g * 128:1024 + (g + 1) * 128],
                                           XNB[:, kt, :],
                                           start=(kt == 0), stop=(kt == KT - 1))
                        return i

                    def eevac(g):
                        return eng.tensor_tensor(BS[8][:, g, DAT], p[g % 2][:, 0:C],
                                                 BS[0][:, 4, 0:C], ALU.mult)

                    def sgroup(g):
                        for kt in range(KT):
                            i = eng.matmul(p[2 + g % 2][:, 0:C],
                                           WBUF[:, kt, g * 128:(g + 1) * 128],
                                           UPB[:, kt, :],
                                           start=(kt == 0), stop=(kt == KT - 1))
                        return i

                    def sevac(g):
                        return eng.tensor_tensor(BS[9][:, g, DAT], p[2 + g % 2][:, 0:C],
                                                 BS[7][:, g, DAT], ALU.add)

                    if _L2 < 5:
                        continue
                    if on("T"):
                        wait("D")
                        wait("S")
                        wait("V")
                    mm_evac(NFT, egroup, "V", eevac)
                    if _L2 < 6:
                        continue
                    if on("T"):
                        wait("S")
                    mm_evac(NFT, sgroup, "V", sevac)
                    if _L2 < 7:
                        continue
                    if on("V"):
                        eng.tensor_add(BS[9][:, :, DAT], BS[9][:, :, DAT], BS[8][:, :, DAT])
                        i = eng.tensor_add(BS[9][:, :, DAT], BS[9][:, :, DAT],
                                           XBUF[:, 0:8, :])
                    else:
                        i = None
                    fin("V", i)
                    if on("G"):
                        wait("V")
                    dma("G", outT[b, hsl, csl].rearrange("(ft q) c -> q ft c", q=128),
                        BS[9][:, :, DAT])

        if _DBG:
            if on("G"):
                wait("V")
                wait("S")
                wait("T")
            dma("G", dbg_rows, d_rows)
            for hf_ in range(2):
                dma("G", dbg_lh[hf_], d_lh[0, hf_, 0])
                dma("G", dbg_lh2[hf_], d_lh[0, hf_, 1])
            dma("G", dbg_sc[0:1, 0:8], CCS[:])
            dma("G", dbg_sc[0:1, 8:16], GSC[:])
            dma("G", dbg_col[:, 0:32], ACC1[:])
            dma("G", dbg_col[:, 32:36], SCAL[:])
            dma("G", dbg_col[:, 36:52], GL0[:])
        if on("G"):
            eng.wait_ge(sems["G"], led["G"])
        return led

    leds = {}

    def attach():
        with nc.Block() as blk:
            @blk.sync
            def _(e):
                leds["D"] = program("D", e)

            @blk.gpsimd
            def _(e):
                leds["G"] = program("G", e)

            @blk.tensor
            def _(e):
                leds["T"] = program("T", e)

            @blk.vector
            def _(e):
                leds["V"] = program("V", e)

            @blk.scalar
            def _(e):
                leds["S"] = program("S", e)

    attach()
    vals = list(leds.values())
    assert all(v == vals[0] for v in vals), f"LEDGER MISMATCH: {leds}"

    ctx.close()
    return nc


_NC_CACHE = {}


def kernel(**inputs):
    x = np.asarray(inputs["L0_attn_hidden_states"], dtype=np.float32)
    Wz = np.asarray(inputs["L0_attn_W_z"], dtype=np.float32)
    bz = np.asarray(inputs["L0_attn_b_z"], dtype=np.float32)
    Wh = np.asarray(inputs["L0_attn_W_h"], dtype=np.float32)
    bh = np.asarray(inputs["L0_attn_b_h"], dtype=np.float32)
    lnw = np.asarray(inputs["L0_attn_ln_w"], dtype=np.float32)
    lnb = np.asarray(inputs["L0_attn_ln_b"], dtype=np.float32)
    h0 = np.asarray(inputs["L0_attn_h0"], dtype=np.float32)
    Wup = np.asarray(inputs["L0_ffn_W_up"], dtype=np.float32)
    Wdn = np.asarray(inputs["L0_ffn_W_down"], dtype=np.float32)
    Wg = np.asarray(inputs["L0_ffn_W_gate"], dtype=np.float32)
    We = np.asarray(inputs["L0_ffn_W_expert"], dtype=np.float32)

    vec = lambda v: np.ascontiguousarray(v.reshape(KT, 128).T)
    com = {
        "wzT": np.ascontiguousarray(Wz.T), "whT": np.ascontiguousarray(Wh.T),
        "wupT": np.ascontiguousarray(Wup.T), "wdnT": np.ascontiguousarray(Wdn.T),
        "wex": np.ascontiguousarray(We), "wgT": np.ascontiguousarray(Wg.T),
        "bzv": vec(bz), "bhv": vec(bh), "lnwv": vec(lnw), "lnbv": vec(lnb),
        "h0v": vec(h0),
        "triC": np.ascontiguousarray(np.triu(np.ones((128, 128), np.float32))),
    }
    xt = np.ascontiguousarray(x.transpose(0, 2, 1))
    in_maps = []
    for c in range(NCORE):
        m = dict(com)
        m["xT"] = np.ascontiguousarray(xt[c * BC:(c + 1) * BC])
        in_maps.append(m)

    if "nc" not in _NC_CACHE:
        _NC_CACHE["nc"] = build_nc()
    nc = _NC_CACHE["nc"]
    res = run_bass_kernel_spmd(nc, in_maps, core_ids=list(range(NCORE)))
    shards = [np.asarray(res.results[c]["outT"]).transpose(0, 2, 1) for c in range(NCORE)]
    return np.ascontiguousarray(np.concatenate(shards, axis=0), dtype=np.float32)


if __name__ == "__main__":
    rng = np.random.RandomState(0)
    fake = {
        "L0_attn_hidden_states": rng.randn(16, T, H).astype(np.float32),
        "L0_attn_W_z": (rng.randn(H, H) * 0.02).astype(np.float32),
        "L0_attn_b_z": rng.randn(H).astype(np.float32),
        "L0_attn_W_h": (rng.randn(H, H) * 0.02).astype(np.float32),
        "L0_attn_b_h": rng.randn(H).astype(np.float32),
        "L0_attn_ln_w": (1 + rng.randn(H)).astype(np.float32),
        "L0_attn_ln_b": rng.randn(H).astype(np.float32),
        "L0_attn_h0": (rng.randn(H) * 0.02).astype(np.float32),
        "L0_ffn_W_up": (rng.randn(H, H) / np.sqrt(H)).astype(np.float32),
        "L0_ffn_W_down": (rng.randn(H, H) / np.sqrt(H)).astype(np.float32),
        "L0_ffn_W_gate": (rng.randn(8, H) / np.sqrt(H)).astype(np.float32),
        "L0_ffn_W_expert": (rng.randn(H, H) / np.sqrt(H)).astype(np.float32),
    }
    out = kernel(**fake)
    print("out", out.shape, out.dtype, np.abs(out).mean())
